# version 06 (final)
# baseline (speedup 1.0000x reference)
import contextlib
import numpy as np
import ml_dtypes
import concourse.bass as bass
import concourse.mybir as mybir
from concourse.bass_utils import run_bass_kernel_spmd

F32 = mybir.dt.float32
BF16 = mybir.dt.bfloat16
I8 = mybir.dt.int8
AF = mybir.ActivationFunctionType
ALU = mybir.AluOpType
AX = mybir.AxisListType

B, T, C = 8, 2048, 1024
NT = T // 128
ND = C // 128
H = 16
DH = 64
NPAIR = 8
INNER0 = 4096
DI = 2048
NSEG = 4
SEG = T // NSEG
MAGIC = 12582912.0
NEG = -1.0e30

ENGINES = ("sy", "ve", "sc", "te", "gp")


class Sem:
    def __init__(self, h):
        self.h = h
        self.n = 0

    def inc(self, k=1):
        self.n += k
        return self.n


class Prog:

    def __init__(self, nc):
        self.nc = nc
        self.q = {e: [] for e in ENGINES}
        self._last = {e: None for e in ENGINES}
        self._es = contextlib.ExitStack()
        self.sems = {}

    def sem(self, name):
        if name not in self.sems:
            h = self._es.enter_context(self.nc.semaphore(name))
            self.sems[name] = Sem(h)
        return self.sems[name]

    def emit(self, eng, fn):
        def wrapped(e, fn=fn, eng=eng, self=self):
            r = fn(e)
            if r is not None and hasattr(r, "then_inc"):
                self._last[eng] = r
        self.q[eng].append(wrapped)

    def dma(self, eng, out, in_, sem, wait=None, slow=False):
        v = sem.inc(16)
        s = sem.h

        def f(e, out=out, in_=in_, wait=wait, slow=slow):
            if wait is not None:
                e.wait_ge(wait[0].h, wait[1])
            e.dma_start(out=out, in_=in_,
                        allow_slow_non_contiguous=slow).then_inc(s, 16)
        self.emit(eng, f)
        return v

    def wait(self, eng, sem, val):
        self.emit(eng, lambda e, s=sem.h, v=val: e.wait_ge(s, v))

    def drain(self, eng, sem):
        v = self.tick(eng, sem)
        self.wait(eng, sem, v)
        return v

    def tick(self, eng, sem, k=1):
        v = sem.inc(k)

        def f(e, eng=eng, s=sem.h, k=k, self=self):
            li = self._last[eng]
            if li is not None:
                li.then_inc(s, k)
                self._last[eng] = None
            else:
                e.sem_inc(s, k)
        self.q[eng].append(f)
        return v

    def finalize(self):
        with self.nc.Block() as block:
            @block.sync
            def _(e):
                for f in self.q["sy"]:
                    f(e)

            @block.vector
            def _(e):
                for f in self.q["ve"]:
                    f(e)

            @block.scalar
            def _(e):
                for f in self.q["sc"]:
                    f(e)

            @block.tensor
            def _(e):
                for f in self.q["te"]:
                    f(e)

            @block.gpsimd
            def _(e):
                for f in self.q["gp"]:
                    f(e)


class Arena:

    def __init__(self, nc, nbytes):
        self.base = nc.alloc_sbuf_tensor("ARENA", [128, nbytes], I8)
        self.nbytes = nbytes
        self.off = 0

    def reset(self):
        self.off = 0

    def take(self, cols, dtype):
        esz = {F32: 4, BF16: 2, I8: 1}[dtype]
        nb = cols * esz
        nb_al = (nb + 63) & ~63
        assert self.off + nb_al <= self.nbytes, (
            f"arena overflow: {self.off}+{nb_al} > {self.nbytes}")
        o = self.off
        self.off += nb_al
        ap = self.base[:, o:o + nb]
        return ap.bitcast(dtype) if dtype != I8 else ap


def build_constants():
    eyeb = np.eye(128).astype(ml_dtypes.bfloat16)
    eyef = np.eye(128, dtype=np.float32)
    pp = np.arange(128)[:, None]
    uu = np.arange(896)[None, :]
    mask = np.where(uu <= pp + 384, 0.0, NEG).astype(np.float32)
    return {"cEYEb": eyeb, "cEYEf": eyef, "cMASK": mask}


PHASES = ("ln0", "attn0", "ffn0", "ln1", "attn1", "ffn1")


def build_program(upto="ffn1", cc_cores=8, a0stop=None):
    nc = bass.Bass("TRN2", target_bir_lowering=False, detect_race_conditions=False)
    p = Prog(nc)
    nphase = PHASES.index(upto)

    hx = nc.dram_tensor("x", [T, C], F32, kind="ExternalInput").ap()
    hout = nc.dram_tensor("out", [T, C], F32, kind="ExternalOutput").ap()
    hWq = nc.dram_tensor("Wq", [H, C, DH], F32, kind="ExternalInput").ap()
    hWk = nc.dram_tensor("Wk", [H, C, DH], F32, kind="ExternalInput").ap()
    hWv = nc.dram_tensor("Wv", [H, C, DH], F32, kind="ExternalInput").ap()
    hWp = nc.dram_tensor("Wp", [C, C], F32, kind="ExternalInput").ap()
    hbp = nc.dram_tensor("bp", [1, C], F32, kind="ExternalInput").ap()
    hwg = nc.dram_tensor("wg", [INNER0, C], F32, kind="ExternalInput").ap()
    hwu = nc.dram_tensor("wu", [INNER0, C], F32, kind="ExternalInput").ap()
    hwd = nc.dram_tensor("wd", [C, INNER0], F32, kind="ExternalInput").ap()
    hwin = nc.dram_tensor("win", [C, 2 * DI], F32, kind="ExternalInput").ap()
    hbin = nc.dram_tensor("bin", [1, 2 * DI], F32, kind="ExternalInput").ap()
    hconvw = nc.dram_tensor("convw", [DI, 4], F32, kind="ExternalInput").ap()
    hconvb = nc.dram_tensor("convb", [1, DI], F32, kind="ExternalInput").ap()
    hwdt = nc.dram_tensor("wdt", [DI, DI], F32, kind="ExternalInput").ap()
    hbdt = nc.dram_tensor("bdt", [1, DI], F32, kind="ExternalInput").ap()
    hdskip = nc.dram_tensor("dskip", [1, DI], F32, kind="ExternalInput").ap()
    hwout = nc.dram_tensor("wout", [DI, C], F32, kind="ExternalInput").ap()
    hbout = nc.dram_tensor("bout", [1, C], F32, kind="ExternalInput").ap()
    hlng = nc.dram_tensor("lng", [1, C], F32, kind="ExternalInput").ap()
    hlnb = nc.dram_tensor("lnb", [1, C], F32, kind="ExternalInput").ap()
    hwi8 = nc.dram_tensor("wi8", [C, C], I8, kind="ExternalInput").ap()
    hws = nc.dram_tensor("ws", [C, 1], F32, kind="ExternalInput").ap()
    hbias1 = nc.dram_tensor("bias1", [1, C], F32, kind="ExternalInput").ap()
    hEYEb = nc.dram_tensor("cEYEb", [128, 128], BF16, kind="ExternalInput").ap()
    hEYEf = nc.dram_tensor("cEYEf", [128, 128], F32, kind="ExternalInput").ap()
    hMASK = nc.dram_tensor("cMASK", [128, 896], F32, kind="ExternalInput").ap()

    sXQ = nc.dram_tensor("sXQ", [128, ND * T], BF16).ap()
    sHI = nc.dram_tensor("sHI", [128, ND * T], BF16).ap()
    sLO = nc.dram_tensor("sLO", [128, ND * T], BF16).ap()
    sWGQ = nc.dram_tensor("sWGQ", [ND, 128, INNER0], BF16).ap()
    sWUQ = nc.dram_tensor("sWUQ", [ND, 128, INNER0], BF16).ap()
    sWDQ = nc.dram_tensor("sWDQ", [32, 128, C], BF16).ap()
    sWINH = nc.dram_tensor("sWINH", [ND, 128, 2 * DI], BF16).ap()
    sWINL = nc.dram_tensor("sWINL", [ND, 128, 2 * DI], BF16).ap()
    sWDTH = nc.dram_tensor("sWDTH", [16, 128, DI], BF16).ap()
    sWDTL = nc.dram_tensor("sWDTL", [16, 128, DI], BF16).ap()
    sWOH = nc.dram_tensor("sWOH", [16, 128, C], BF16).ap()
    sWOL = nc.dram_tensor("sWOL", [16, 128, C], BF16).ap()
    sCCI = nc.dram_tensor("sCCI", [1, 32], F32).ap()
    sCCO = nc.dram_tensor("sCCO", [1, 32], F32).ap()
    sW8T = nc.dram_tensor("sW8T", [ND, 128, C], BF16).ap()

    sb = nc.alloc_sbuf_tensor
    XT = [sb(f"XT{i}", [128, C], F32) for i in range(NT)]
    CEYEb = sb("CEYEb", [128, 128], BF16)
    CEYEf = sb("CEYEf", [128, 128], F32)
    CPOOL = sb("CPOOL", [128, 1280], F32)
    CMASK = CPOOL[:, 0:896]
    ONESF = CPOOL[:, 896:1024]
    HALVES = CPOOL[:, 1024:1280]
    EPS = sb("EPS", [128, 1], F32)
    STAT = sb("STAT", [128, 80], F32)
    SCR = sb("SCR", [128, C], F32)
    SCR2 = sb("SCR2", [128, C], F32)
    rem = nc.sbuf_bytes_remaining
    AR = Arena(nc, (rem - 1024) & ~63)
    PSALL = nc.alloc_psum_tensor("PSALL", [128, 4096], F32)

    def bank(k, lo=0, hi=512):
        return PSALL[:, 512 * k + lo:512 * k + hi]

    def bankb(k, lo=0, hi=1024):
        return PSALL[:].bitcast(BF16)[:, 1024 * k + lo:1024 * k + hi]

    def fence_mm():
        p.emit("te", lambda e: e.matmul(bank(7, 0, 128), CEYEf[:], CEYEf[:],
                                        start=True, stop=True))

    s_wx = p.sem("s_wx")
    s_gd = p.sem("s_gd")
    s_te = p.sem("s_te")
    s_ve = p.sem("s_ve")
    s_sc = p.sem("s_sc")
    s_st = p.sem("s_st")
    s_cc = p.sem("s_cc")

    for i in range(NT):
        p.dma("sy", XT[i][:], hx[128 * i:128 * (i + 1), :], s_wx)
    p.dma("sy", CEYEb[:], hEYEb, s_wx)
    p.dma("sy", CEYEf[:], hEYEf, s_wx)
    p.dma("sy", CMASK[:], hMASK, s_wx)
    v_load = s_wx.n
    p.wait("ve", s_wx, v_load)
    p.emit("ve", lambda e: e.memset(EPS[:], 1e-5))
    p.emit("ve", lambda e: e.memset(STAT[:], 0.0))
    p.emit("ve", lambda e: e.memset(ONESF[:], 1.0))
    p.emit("ve", lambda e: e.memset(HALVES[0:64, 0:128], 1.0))
    p.emit("ve", lambda e: e.memset(HALVES[64:128, 0:128], 0.0))
    p.emit("ve", lambda e: e.memset(HALVES[0:64, 128:256], 0.0))
    p.emit("ve", lambda e: e.memset(HALVES[64:128, 128:256], 1.0))
    p.wait("sc", s_wx, v_load)
    p.wait("te", s_wx, v_load)
    p.emit("te", lambda e: e.matmul(bank(0, 0, 128), CEYEf[:], CEYEf[:],
                                    start=True, stop=True))

    def emit_ln_stats(m0, r0, s0=None, absmax_only=False):
        p.wait("sc", s_ve, s_ve.n)
        for t in range(NT):
            p.emit("ve", lambda e, t=t: e.reduce_sum(
                STAT[:, m0 + t:m0 + t + 1], XT[t][:], AX.X))
            p.emit("sc", lambda e, t=t: e.activation(
                out=SCR[:], in_=XT[t][:], func=AF.Square,
                accum_out=STAT[:, 48 + t:48 + t + 1]))
            if s0 is not None:
                p.emit("ve", lambda e, t=t: e.reduce_max(
                    STAT[:, s0 + t:s0 + t + 1], XT[t][:], AX.X))
                p.emit("ve", lambda e, t=t: e.tensor_reduce(
                    SCR2[:, 16 + t:16 + t + 1], XT[t][:], AX.X, ALU.min))
        vsc = p.tick("sc", s_sc)
        p.drain("ve", s_ve)
        p.wait("ve", s_sc, vsc)
        M = STAT[:, m0:m0 + 16]
        R = STAT[:, r0:r0 + 16]
        E2 = STAT[:, 48:64]
        V = SCR2[:, 32:48]
        p.emit("ve", lambda e: e.tensor_scalar(M, M, 1.0 / C, None, ALU.mult))
        p.emit("ve", lambda e: e.tensor_scalar(
            E2, E2, 1.0 / C, None, ALU.mult))
        p.emit("ve", lambda e: e.tensor_tensor(V, M, M, ALU.mult))
        if s0 is not None:
            p.emit("ve", lambda e: e.tensor_tensor(
                STAT[:, s0:s0 + 16], STAT[:, s0:s0 + 16], M, ALU.subtract))
        p.emit("ve", lambda e: e.tensor_tensor(V, E2, V, ALU.subtract))
        vve = p.tick("ve", s_ve)
        p.wait("sc", s_ve, vve)
        p.emit("sc", lambda e: e.activation(
            out=V, in_=V, func=AF.Sqrt, bias=EPS[:]))
        vsc = p.tick("sc", s_sc)
        p.wait("ve", s_sc, vsc)
        p.emit("ve", lambda e: e.reciprocal(R, V))
        p.drain("ve", s_ve)
        if s0 is not None:
            S = STAT[:, s0:s0 + 16]
            hv = ((slice(s0, s0 + 8), slice(16, 24), slice(r0, r0 + 8)),
                  (slice(s0 + 8, s0 + 16), slice(24, 32),
                   slice(r0 + 8, r0 + 16)))
            for ss, mm2, rr in hv:
                mcols = slice(m0 + mm2.start - 16, m0 + mm2.stop - 16)
                p.emit("ve", lambda e, mm2=mm2, mcols=mcols: e.tensor_tensor(
                    SCR2[:, mm2], STAT[:, mcols], SCR2[:, mm2], ALU.subtract))
            for ss, mm2, rr in hv:
                p.emit("ve", lambda e, ss=ss, mm2=mm2: e.tensor_tensor(
                    STAT[:, ss], STAT[:, ss], SCR2[:, mm2], ALU.max))
            for ss, mm2, rr in hv:
                p.emit("ve", lambda e, ss=ss, rr=rr: e.tensor_tensor(
                    STAT[:, ss], STAT[:, ss], STAT[:, rr], ALU.mult))
            if not absmax_only:
                for ss, mm2, rr in hv:
                    p.emit("ve", lambda e, ss=ss: e.tensor_scalar(
                        STAT[:, ss], STAT[:, ss], 1e-5, None, ALU.max))
                for ss, mm2, rr in hv:
                    a0 = 64 + ss.start - s0
                    p.emit("ve", lambda e, ss=ss, a0=a0: e.tensor_scalar(
                        STAT[:, a0:a0 + 8], STAT[:, ss], 1.0 / 127.0, None,
                        ALU.mult))
                p.drain("ve", s_ve)
                p.emit("ve", lambda e: e.reciprocal(S, S))
                p.drain("ve", s_ve)
                for ss, mm2, rr in hv:
                    p.emit("ve", lambda e, ss=ss: e.tensor_scalar(
                        STAT[:, ss], STAT[:, ss], 127.0, None, ALU.mult))
            else:
                p.drain("ve", s_ve)

    def emit_split_fm(HIFM, LOFM, STG, m0, r0, s0, xq_hbm):
        state = {"free": 0, "lastq": 0}
        for t in range(NT):
            mcol = STAT[:, m0 + t:m0 + t + 1]
            rcol = STAT[:, r0 + t:r0 + t + 1]
            p.wait("ve", s_te, s_te.n)
            ha, hb = slice(0, 512), slice(512, 1024)
            for hs in (ha, hb):
                p.emit("ve", lambda e, t=t, mcol=mcol, rcol=rcol, hs=hs:
                       e.tensor_scalar(SCR[:, hs], XT[t][:, hs], mcol, rcol,
                                       ALU.subtract, ALU.mult))
            if s0 is not None:
                scol = STAT[:, s0 + t:s0 + t + 1]
                for hs in (ha, hb):
                    p.emit("ve", lambda e, scol=scol, hs=hs: e.tensor_scalar(
                        SCR2[:, hs], SCR[:, hs], scol, MAGIC,
                        ALU.mult, ALU.add))
                for hs in (ha, hb):
                    p.emit("ve", lambda e, hs=hs: e.tensor_scalar(
                        SCR2[:, hs], SCR2[:, hs], -MAGIC, -128.0,
                        ALU.add, ALU.max))
                for hs in (ha, hb):
                    p.emit("ve", lambda e, hs=hs: e.tensor_scalar(
                        SCR2[:, hs], SCR2[:, hs], 127.0, None, ALU.min))
            vprep = p.tick("ve", s_ve)
            free = max(state["free"], vprep)
            for d0 in range(0, ND, 7):
                ds = list(range(d0, min(d0 + 7, ND)))
                p.wait("te", s_ve, free)
                for j, d in enumerate(ds):
                    p.emit("te", lambda e, j=j, d=d: e.transpose(
                        bank(j, 0, 128), SCR[:, 128 * d:128 * (d + 1)],
                        CEYEf[:]))
                fence_mm()
                vte = p.tick("te", s_te)
                p.wait("ve", s_te, vte)
                for j, d in enumerate(ds):
                    def evh(e, j=j, d=d, t=t):
                        return e.tensor_copy(
                            out=HIFM[:, T * d + 128 * t:T * d + 128 * (t + 1)],
                            in_=bank(j, 0, 128))
                    p.emit("ve", evh)
                for j, d in enumerate(ds):
                    def evl(e, j=j, d=d, t=t):
                        return e.tensor_tensor(
                            LOFM[:, T * d + 128 * t:T * d + 128 * (t + 1)],
                            bank(j, 0, 128),
                            HIFM[:, T * d + 128 * t:T * d + 128 * (t + 1)],
                            ALU.subtract)
                    p.emit("ve", evl)
                free = p.tick("ve", s_ve)
            state["free"] = free
            if s0 is not None:
                for d0 in range(0, ND, 7):
                    ds = list(range(d0, min(d0 + 7, ND)))
                    p.wait("te", s_ve, state["free"])
                    for j, d in enumerate(ds):
                        p.emit("te", lambda e, j=j, d=d: e.transpose(
                            bank(j, 0, 128), SCR2[:, 128 * d:128 * (d + 1)],
                            CEYEf[:]))
                    fence_mm()
                    vte = p.tick("te", s_te)
                    p.wait("ve", s_te, vte)
                    if d0 == 0:
                        p.wait("ve", s_gd, state["lastq"])
                    for j, d in enumerate(ds):
                        def evq(e, j=j, d=d, STG=STG):
                            return e.tensor_copy(
                                out=STG[:, 128 * d:128 * (d + 1)],
                                in_=bank(j, 0, 128))
                        p.emit("ve", evq)
                    state["free"] = p.tick("ve", s_ve)
                vv = state["free"]
                dst_ap = xq_hbm.rearrange("p (d t2) -> p d t2", d=ND)[
                    :, :, 128 * t:128 * (t + 1)]
                src_ap2 = STG[:, 0:1024].rearrange("p (d c) -> p d c", d=ND)
                state["lastq"] = p.dma("gp", dst_ap, src_ap2, s_gd,
                                       wait=(s_ve, vv))

    AR.reset()
    HIFM = AR.take(ND * T, BF16)
    LOFM = AR.take(ND * T, BF16)
    STG4 = AR.take(4096, BF16)
    emit_ln_stats(0, 16, 32)
    emit_split_fm(HIFM, LOFM, STG4, 0, 16, 32, sXQ)

    def emit_attn0():
        ar0 = AR.off
        Q0HI = AR.take(T, BF16)
        Q0LO = AR.take(T, BF16)
        Q1HI = AR.take(T, BF16)
        Q1LO = AR.take(T, BF16)
        KHI = AR.take(T, BF16)
        KLO = AR.take(T, BF16)
        VFM = AR.take(T, F32)
        PBC = SCR2[:]
        PT0 = AR.take(T, F32)
        PT1 = AR.take(T, F32)
        OP = AR.take(T, F32)
        WH = STG4[:, 0:1024]
        WL = STG4[:, 1024:2048]
        WSF = SCR[:]

        if a0stop == "dbgln":
            p.wait("ve", s_gd, s_gd.n)
            p.emit("ve", lambda e: e.tensor_copy(
                out=XT[8][:, 0:64], in_=STAT[:]))
            p.emit("ve", lambda e: e.tensor_copy(
                out=XT[9][:], in_=HIFM.bitcast(F32)[:, 0:1024]))
            p.emit("ve", lambda e: e.tensor_copy(
                out=XT[10][:], in_=LOFM.bitcast(F32)[:, 0:1024]))
            p.tick("ve", s_ve)
            AR.off = ar0
            return
        vv = p.tick("ve", s_ve)
        vbp = p.dma("gp", SCR2[:], hbp.partition_broadcast(128), s_gd,
                    wait=(s_ve, vv))
        p.wait("ve", s_gd, vbp)
        for i in range(NT):
            p.emit("ve", lambda e, i=i: e.tensor_tensor(
                XT[i][:], XT[i][:], SCR2[:], ALU.add))
        p.emit("ve", lambda e: e.memset(Q0HI[64:128, :], 0.0))
        p.emit("ve", lambda e: e.memset(Q0LO[64:128, :], 0.0))
        p.emit("ve", lambda e: e.memset(Q1HI[0:64, :], 0.0))
        p.emit("ve", lambda e: e.memset(Q1LO[0:64, :], 0.0))
        p.tick("ve", s_ve)
        if a0stop == "bp":
            AR.off = ar0
            return

        def load_pair_weight(hW, pp):
            p.wait("sy", s_ve, s_ve.n)
            vlast = 0
            for dc in range(ND):
                src = hW[2 * pp:2 * pp + 2,
                         128 * dc:128 * (dc + 1), 0:DH].transpose([1, 0, 2])
                vlast = p.dma("sy", WSF[:, 128 * dc:128 * (dc + 1)].rearrange(
                    "p (h d) -> p h d", h=2), src, s_wx)
            p.wait("ve", s_wx, vlast)
            p.wait("ve", s_te, s_te.n)
            for hs in (slice(0, 512), slice(512, 1024)):
                p.emit("ve", lambda e, hs=hs: e.tensor_copy(
                    out=WH[:, hs], in_=WSF[:, hs]))
            for hs in (slice(0, 512), slice(512, 1024)):
                p.emit("ve", lambda e, hs=hs: e.scalar_tensor_tensor(
                    WL[:, hs], WH[:, hs], -1.0, WSF[:, hs],
                    ALU.mult, ALU.add))
            return p.tick("ve", s_ve)

        for pp in range(1 if a0stop else NPAIR):
            vqred = 0
            wlist = ((hWq, None, None), (hWk, KHI, KLO), (hWv, VFM, None))
            sub = a0stop in ("ld1", "mm1", "sq1", "cs1")
            for wi, (hW, DHI2, DLO2) in enumerate(wlist[:1] if sub else wlist):
                vw = load_pair_weight(hW, pp)
                if a0stop == "ld1":
                    AR.off = ar0
                    return
                p.wait("te", s_ve, vw)
                for tc in range(1 if sub else 4):
                    bk = 4 + (tc % 2)
                    for ip, (wop, xop) in enumerate(
                            ((WH, HIFM), (WH, LOFM), (WL, HIFM))):
                        for dc in range(ND):
                            p.emit("te", lambda e, bk=bk, wop=wop, xop=xop,
                                   dc=dc, tc=tc, ip=ip: e.matmul(
                                bank(bk),
                                wop[:, 128 * dc:128 * (dc + 1)],
                                xop[:, T * dc + 512 * tc:T * dc + 512 * (tc + 1)],
                                start=(ip == 0 and dc == 0),
                                stop=(ip == 2 and dc == ND - 1)))
                    vte = p.tick("te", s_te)
                    p.wait("ve", s_te, vte)
                    if wi < 2:
                        if wi == 0:
                            for hh, DH_ in enumerate((Q0HI, Q1HI)):
                                lo_, hi_ = 64 * hh, 64 * (hh + 1)
                                p.emit("ve", lambda e, bk=bk, D=DH_, tc=tc,
                                       lo_=lo_, hi_=hi_: e.tensor_copy(
                                    out=D[lo_:hi_, 512 * tc:512 * (tc + 1)],
                                    in_=bank(bk)[lo_:hi_, :]))
                            for hh, (DH_, DL_) in enumerate(
                                    ((Q0HI, Q0LO), (Q1HI, Q1LO))):
                                lo_, hi_ = 64 * hh, 64 * (hh + 1)
                                p.emit("ve", lambda e, bk=bk, D=DH_, D2=DL_,
                                       tc=tc, lo_=lo_, hi_=hi_:
                                       e.scalar_tensor_tensor(
                                    D2[lo_:hi_, 512 * tc:512 * (tc + 1)],
                                    D[lo_:hi_, 512 * tc:512 * (tc + 1)], -1.0,
                                    bank(bk)[lo_:hi_, :], ALU.mult, ALU.add))
                            p.emit("ve", lambda e, tc=tc: e.tensor_tensor(
                                SCR2[:, 512:1024],
                                Q0HI[:, 512 * tc:512 * (tc + 1)],
                                Q1HI[:, 512 * tc:512 * (tc + 1)], ALU.add))
                        else:
                            for cs in (slice(0, 256), slice(256, 512)):
                                p.emit("ve", lambda e, bk=bk, D=DHI2, tc=tc,
                                       cs=cs: e.tensor_copy(
                                    out=D[:, 512 * tc + cs.start:
                                          512 * tc + cs.stop],
                                    in_=bank(bk)[:, cs]))
                            for cs in (slice(0, 256), slice(256, 512)):
                                p.emit("ve", lambda e, bk=bk, D=DHI2, D2=DLO2,
                                       tc=tc, cs=cs: e.scalar_tensor_tensor(
                                    D2[:, 512 * tc + cs.start:
                                       512 * tc + cs.stop],
                                    D[:, 512 * tc + cs.start:
                                      512 * tc + cs.stop], -1.0,
                                    bank(bk)[:, cs], ALU.mult, ALU.add))
                        vve = p.tick("ve", s_ve)
                        if a0stop == "mm1":
                            continue
                        p.wait("sc", s_ve, vve)
                        if wi == 0:
                            p.emit("sc", lambda e: e.activation(
                                out=SCR2[:, 0:512], in_=SCR2[:, 512:1024],
                                func=AF.Square))
                        else:
                            p.emit("sc", lambda e, tc=tc: e.activation(
                                out=SCR2[:, 0:512],
                                in_=KHI[:, 512 * tc:512 * (tc + 1)],
                                func=AF.Square))
                        vsc = p.tick("sc", s_sc)
                        if a0stop == "sq1":
                            continue
                        p.wait("te", s_sc, vsc)
                        p.wait("te", s_ve, vqred)
                        for hh in range(2):
                            p.emit("te", lambda e, hh=hh: e.matmul(
                                bank(2 + hh),
                                HALVES[:, 128 * hh:128 * (hh + 1)],
                                SCR2[:, 0:512], start=True, stop=True))
                        vte2 = p.tick("te", s_te)
                        p.wait("ve", s_te, vte2)
                        for hh in range(2):
                            col = 48 + 4 * (2 * wi + hh) + tc
                            p.emit("ve", lambda e, hh=hh, col=col:
                                   e.reduce_max(STAT[:, col:col + 1],
                                                bank(2 + hh), AX.X))
                        vqred = p.tick("ve", s_ve)
                    else:
                        p.emit("ve", lambda e, bk=bk, tc=tc: e.tensor_copy(
                            out=VFM[:, 512 * tc:512 * (tc + 1)], in_=bank(bk)))
                        vve = p.tick("ve", s_ve)
                        p.wait("te", s_ve, vve)
            if a0stop in ("proj", "mm1", "sq1", "cs1"):
                AR.off = ar0
                return
            if a0stop == "dbgproj":
                for i, bsrc in enumerate((Q0HI, Q0LO, Q1HI, Q1LO, KHI, KLO)):
                    p.emit("ve", lambda e, i=i, bsrc=bsrc: e.tensor_copy(
                        out=XT[i][:], in_=bsrc.bitcast(F32)))
                p.emit("ve", lambda e: e.tensor_copy(
                    out=XT[6][:], in_=VFM[:, 0:1024]))
                p.emit("ve", lambda e: e.tensor_copy(
                    out=XT[7][:], in_=VFM[:, 1024:2048]))
                p.emit("ve", lambda e: e.tensor_copy(
                    out=XT[8][:, 0:64], in_=STAT[:]))
                p.tick("ve", s_ve)
                AR.off = ar0
                return
            p.drain("ve", s_ve)
            for g, c0 in enumerate((48, 52, 56, 60)):
                p.emit("ve", lambda e, g=g, c0=c0: e.reduce_max(
                    SCR2[:, g:g + 1], STAT[:, c0:c0 + 4], AX.X))
            p.drain("ve", s_ve)
            for hh in range(2):
                p.emit("ve", lambda e, hh=hh: e.tensor_tensor(
                    SCR2[:, 4 + hh:5 + hh], SCR2[:, hh:hh + 1],
                    SCR2[:, 2 + hh:3 + hh], ALU.mult))
            vve = p.tick("ve", s_ve)
            p.wait("sc", s_ve, vve)
            p.emit("sc", lambda e: e.activation(
                out=SCR2[:, 4:6], in_=SCR2[:, 4:6], func=AF.Sqrt))
            vsc = p.tick("sc", s_sc)
            p.wait("ve", s_sc, vsc)
            p.emit("ve", lambda e: e.tensor_scalar(
                STAT[:, 48:50], SCR2[:, 4:6], -1.0 / 32.0, None, ALU.mult))
            p.tick("ve", s_ve)
            if a0stop == "bound":
                AR.off = ar0
                return

            for g in range(4):
                vve = p.tick("ve", s_ve)
                p.wait("te", s_ve, vve)
                for j in range(4):
                    p.emit("te", lambda e, g=g, j=j: e.transpose(
                        bank(j, 0, 128),
                        VFM[:, 128 * (4 * g + j):128 * (4 * g + j + 1)],
                        CEYEf[:]))
                fence_mm()
                vte = p.tick("te", s_te)
                p.wait("ve", s_te, vte)
                for j in range(4):
                    def evacv(e, g=g, j=j):
                        return e.tensor_copy(
                            out=VFM[:, 512 * g + 128 * j:512 * g + 128 * (j + 1)],
                            in_=bank(j, 0, 128))
                    p.emit("ve", evacv)
            vVdone = p.tick("ve", s_ve)
            p.wait("te", s_ve, vVdone)
            if a0stop == "vt":
                AR.off = ar0
                return

            ptdone = 0
            for qi in range(NT - 1, -1, -1):
                ncH = qi // 4 + 1
                for c in range(ncH):
                    for ip in range(3):
                        for hh in range(2):
                            qop = ((Q0HI, Q1HI), (Q0HI, Q1HI),
                                   (Q0LO, Q1LO))[ip][hh]
                            kop = (KHI, KLO, KHI)[ip]
                            p.emit("te", lambda e, qop=qop, kop=kop, hh=hh,
                                   qi=qi, c=c, ip=ip: e.matmul(
                                bank(hh),
                                qop[:, 128 * qi:128 * (qi + 1)],
                                kop[:, 512 * c:512 * (c + 1)],
                                start=(ip == 0), stop=(ip == 2)))
                    vte = p.tick("te", s_te)
                    if a0stop == "s1":
                        p.wait("ve", s_te, vte)
                        p.emit("ve", lambda e: e.tensor_copy(
                            out=SCR2[:, 0:512], in_=bank(0)))
                        p.emit("ve", lambda e: e.tensor_copy(
                            out=SCR2[:, 512:1024], in_=bank(1)))
                        p.tick("ve", s_ve)
                        AR.off = ar0
                        return
                    diag = (c == qi // 4)
                    p.wait("ve", s_te, vte)
                    if diag:
                        for hh in range(2):
                            p.emit("ve", lambda e, hh=hh, qi=qi: e.tensor_tensor(
                                SCR[:, 512 * hh:512 * (hh + 1)], bank(hh),
                                CMASK[:, 384 - 128 * (qi % 4):
                                      896 - 128 * (qi % 4)],
                                ALU.add))
                    vve = p.tick("ve", s_ve)
                    p.wait("sc", s_ve, vve)
                    p.wait("sc", s_te, ptdone)
                    for hh in range(2):
                        src = ((lambda hh: SCR[:, 512 * hh:512 * (hh + 1)])(hh)
                               if diag else bank(hh))
                        p.emit("sc", lambda e, hh=hh, c=c, src=src: e.activation(
                            out=PBC[:, 512 * hh:512 * (hh + 1)],
                            in_=src, func=AF.Exp,
                            bias=STAT[:, 48 + hh:49 + hh],
                            scale=1.0 / 32.0,
                            accum_out=STAT[:, 50 + 4 * hh + c:51 + 4 * hh + c]))
                    vsc = p.tick("sc", s_sc)
                    if a0stop == "e1":
                        p.wait("ve", s_sc, vsc)
                        p.tick("ve", s_ve)
                        AR.off = ar0
                        return
                    p.wait("te", s_sc, vsc)
                    PTBK = (2, 4, 5, 6)
                    for hh, PT in ((0, PT0), (1, PT1)):
                        nblk = min(4, qi + 1 - 4 * c)
                        for j in range(nblk):
                            p.emit("te", lambda e, hh=hh, j=j: e.transpose(
                                bank(PTBK[j], 0, 128),
                                PBC[:, 512 * hh + 128 * j:512 * hh + 128 * (j + 1)],
                                CEYEf[:]))
                        fence_mm()
                        vte = p.tick("te", s_te)
                        p.wait("ve", s_te, vte)
                        for j in range(nblk):
                            def evacp(e, PT=PT, c=c, j=j):
                                return e.tensor_copy(
                                    out=PT[:, 512 * c + 128 * j:
                                           512 * c + 128 * (j + 1)],
                                    in_=bank(PTBK[j], 0, 128))
                            p.emit("ve", evacp)
                        vve = p.tick("ve", s_ve)
                        p.wait("te", s_ve, vve)
                        ptdone = vte
                    p.wait("te", s_sc, vsc)
                    if (a0stop and a0stop.startswith("p")
                            and c == int(a0stop[1:]) - 1):
                        AR.off = ar0
                        return
                if a0stop == "sc1":
                    AR.off = ar0
                    return
                p.emit("ve", lambda e, ncH=ncH: e.reduce_sum(
                    SCR2[:, 0:1], STAT[:, 50:50 + ncH], AX.X))
                p.emit("ve", lambda e, ncH=ncH: e.reduce_sum(
                    SCR2[:, 1:2], STAT[:, 54:54 + ncH], AX.X))
                p.drain("ve", s_ve)
                p.emit("ve", lambda e: e.reciprocal(
                    SCR2[:, 2:4], SCR2[:, 0:2]))
                p.drain("ve", s_ve)
                p.emit("ve", lambda e: e.tensor_scalar(
                    SCR[:, 0:64], ONESF[:, 0:64], SCR2[:, 2:3], None,
                    ALU.mult))
                p.emit("ve", lambda e: e.tensor_scalar(
                    SCR[:, 64:128], ONESF[:, 0:64], SCR2[:, 3:4], None,
                    ALU.mult))
                vve = p.tick("ve", s_ve)
                p.wait("te", s_ve, vve)
                p.emit("te", lambda e: e.transpose(
                    bank(3, 0, 128), SCR[:, 0:128], CEYEf[:]))
                fence_mm()
                vteR = p.tick("te", s_te)
                for hh, PT, pvb in ((0, PT0, 2), (1, PT1, 0)):
                    for sc_ in range(qi + 1):
                        p.emit("te", lambda e, PT=PT, pvb=pvb, sc_=sc_, qi=qi:
                               e.matmul(
                            bank(pvb, 0, 128),
                            VFM[:, 128 * sc_:128 * (sc_ + 1)],
                            PT[:, 128 * sc_:128 * (sc_ + 1)],
                            start=(sc_ == 0), stop=(sc_ == qi)))
                vte = p.tick("te", s_te)
                p.wait("ve", s_te, vteR)
                p.emit("ve", lambda e: e.tensor_copy(
                    out=SCR2[:, 512:640], in_=bank(3, 0, 128)))
                p.wait("ve", s_te, vte)
                for hh, pvb, c0 in ((0, 2, 640), (1, 0, 768)):
                    p.emit("ve", lambda e, pvb=pvb, c0=c0: e.tensor_tensor(
                        SCR2[:, c0:c0 + 128], bank(pvb, 0, 128),
                        SCR2[:, 512:640], ALU.mult))
                for hh, c0 in ((0, 640), (1, 768)):
                    def evaco(e, qi=qi, hh=hh, c0=c0):
                        lo_, hi_ = 64 * hh, 64 * (hh + 1)
                        e.tensor_copy(
                            out=OP[lo_:hi_, 128 * qi:128 * (qi + 1)],
                            in_=SCR2[lo_:hi_, c0:c0 + 128])
                    p.emit("ve", evaco)
                vve = p.tick("ve", s_ve)
                p.wait("te", s_ve, vve)
                if a0stop == "qi1":
                    AR.off = ar0
                    return

            p.wait("sy", s_ve, s_ve.n)
            v1 = p.dma("sy", SCR[:], hWp[128 * pp:128 * (pp + 1), :], s_wx)
            p.wait("te", s_wx, v1)
            vfree = 0
            for tt in range(NT):
                p.wait("te", s_ve, vfree)
                for oc in range(2):
                    p.emit("te", lambda e, oc=oc, tt=tt: e.matmul(
                        bank(4 + oc),
                        OP[:, 128 * tt:128 * (tt + 1)],
                        SCR[:, 512 * oc:512 * (oc + 1)],
                        start=True, stop=True))
                vte = p.tick("te", s_te)
                p.wait("ve", s_te, vte)
                for oc in range(2):
                    p.emit("ve", lambda e, tt=tt, oc=oc: e.tensor_tensor(
                        XT[tt][:, 512 * oc:512 * (oc + 1)],
                        XT[tt][:, 512 * oc:512 * (oc + 1)],
                        bank(4 + oc), ALU.add))
                vfree = p.tick("ve", s_ve)

        AR.off = ar0
        return

    if nphase >= 1:
        emit_attn0()

    def emit_wprep(hw_src, n_rowtiles, width, dst, sw_col):
        p.wait("sy", s_ve, s_ve.n)
        p.wait("sy", s_gd, s_gd.n)
        LDB = AR.take(2 * 1024, F32)
        nw = width // 1024
        p.emit("ve", lambda e: e.memset(SCR2[:, 0:512], 0.0))
        p.drain("ve", s_ve)
        vld = [0, 0]
        vslot = [0, 0]
        for i in range(n_rowtiles * nw):
            b = i % 2
            rt, cg = divmod(i, nw)
            p.wait("sy", s_ve, vslot[b])
            vld[b] = p.dma("sy", LDB[:, 1024 * b:1024 * (b + 1)],
                           hw_src[128 * rt:128 * (rt + 1),
                                  1024 * cg:1024 * (cg + 1)], s_wx)
            p.wait("ve", s_wx, vld[b])
            p.emit("ve", lambda e, b=b, i=i: e.reduce_sum(
                SCR2[:, i % 512:i % 512 + 1], LDB[:, 1024 * b:1024 * (b + 1)],
                AX.X, apply_absolute_value=True))
            vslot[b] = p.tick("ve", s_ve)
        vred = max(vslot)
        p.wait("te", s_ve, vred)
        p.emit("te", lambda e: e.matmul(
            bank(3), ONESF[:], SCR2[:, 0:512], start=True, stop=True))
        vte = p.tick("te", s_te)
        p.wait("ve", s_te, vte)
        p.emit("ve", lambda e: e.reduce_sum(
            SCR2[:, 512:513], bank(3), AX.X))
        p.drain("ve", s_ve)
        NW = float(n_rowtiles * 128 * width)
        p.emit("ve", lambda e, NW=NW: e.tensor_scalar(
            SCR2[:, 513:514], SCR2[:, 512:513], 1.0 / NW, 1e-5,
            ALU.mult, ALU.max))
        p.drain("ve", s_ve)
        p.emit("ve", lambda e, sw_col=sw_col: e.reciprocal(
            STAT[:, sw_col:sw_col + 1], SCR2[:, 513:514]))
        p.emit("ve", lambda e, sw_col=sw_col: e.tensor_copy(
            out=STAT[:, sw_col + 8:sw_col + 9], in_=SCR2[:, 513:514]))
        p.drain("ve", s_ve)
        vve = p.tick("ve", s_ve)
        QB = AR.take(width, F32)
        QS = AR.take(width, BF16)
        vqs = 0
        for rt in range(n_rowtiles):
            b = rt % 2
            vld2 = 0
            for cg in range(nw):
                p.wait("sy", s_ve, vve)
                vld2 = p.dma("sy", LDB[:, 1024 * b:1024 * (b + 1)]
                             if nw == 1 else LDB[:, 0:1024],
                             hw_src[128 * rt:128 * (rt + 1),
                                    1024 * cg:1024 * (cg + 1)], s_wx)
                p.wait("ve", s_wx, vld2)
                sl = slice(1024 * cg, 1024 * (cg + 1))
                hvs = (slice(0, 512), slice(512, 1024))
                for hs in hvs:
                    p.emit("ve", lambda e, b=b, sw_col=sw_col, nw=nw, hs=hs:
                           e.tensor_scalar(
                        SCR[:, hs],
                        (LDB[:, 0:1024] if nw > 1
                         else LDB[:, 1024 * b:1024 * (b + 1)])[:, hs],
                        STAT[:, sw_col:sw_col + 1], MAGIC,
                        ALU.mult, ALU.add))
                for hs in hvs:
                    p.emit("ve", lambda e, hs=hs: e.tensor_scalar(
                        SCR[:, hs], SCR[:, hs], -MAGIC, -1.0,
                        ALU.add, ALU.max))
                for hs in hvs:
                    p.emit("ve", lambda e, QB=QB, sl=sl, hs=hs:
                           e.tensor_scalar(
                        QB[:, sl][:, hs], SCR[:, hs], 1.0, None, ALU.min))
                vve = p.tick("ve", s_ve)
            nblk = width // 128
            p.wait("ve", s_gd, vqs)
            for g0 in range(0, nblk, 7):
                gb = min(7, nblk - g0)
                p.wait("te", s_ve, vve)
                for j in range(gb):
                    p.emit("te", lambda e, g0=g0, j=j, QB=QB: e.transpose(
                        bank(j, 0, 128),
                        QB[:, 128 * (g0 + j):128 * (g0 + j + 1)], CEYEf[:]))
                fence_mm()
                vte = p.tick("te", s_te)
                p.wait("ve", s_te, vte)
                for j in range(gb):
                    def evacw(e, g0=g0, j=j, QS=QS):
                        return e.tensor_copy(
                            out=QS[:, 128 * (g0 + j):128 * (g0 + j + 1)],
                            in_=bank(j, 0, 128))
                    p.emit("ve", evacw)
                vve = p.tick("ve", s_ve)
            dst_ap = dst[:, :, 128 * rt:128 * (rt + 1)].transpose([1, 0, 2])
            src_ap = QS[:].rearrange("p (g c) -> p g c", g=nblk)
            vqs = p.dma("gp", dst_ap, src_ap, s_gd, wait=(s_ve, vve))
        p.wait("ve", s_gd, vqs)
        vve = p.tick("ve", s_ve)
        return vve

    def emit_ffn0():
        AR.reset()
        emit_wprep(hwg, 32, 1024, sWGQ, 48)
        AR.reset()
        emit_wprep(hwu, 32, 1024, sWUQ, 49)
        AR.reset()
        emit_wprep(hwd, 8, 4096, sWDQ, 50)
        AR.reset()

        if a0stop == "dbgwq":
            p.wait("sy", s_ve, s_ve.n)
            v1 = p.dma("sy", XT[0][:].bitcast(BF16), sWGQ[0, :, 0:2048], s_wx)
            v1 = p.dma("sy", XT[1][:].bitcast(BF16), sWGQ[0, :, 2048:4096],
                       s_wx)
            v1 = p.dma("sy", XT[2][:].bitcast(BF16)[:, 0:1024],
                       sWDQ[0, :, 0:1024], s_wx)
            v1 = p.dma("sy", XT[4][:].bitcast(BF16), sXQ[:, 0:2048], s_wx)
            v1 = p.dma("sy", XT[5][:].bitcast(BF16), sXQ[:, 2048:4096], s_wx)
            p.wait("ve", s_wx, v1)
            p.emit("ve", lambda e: e.tensor_copy(
                out=XT[3][:, 0:80], in_=STAT[:]))
            p.tick("ve", s_ve)
            return

        XQF = AR.take(ND * T, BF16)
        GT = AR.take(INNER0, F32)
        UT = AR.take(INNER0, F32)
        XQ2 = AR.take(INNER0, BF16)
        XQ2F = AR.take(INNER0, BF16)
        WGQ = AR.take(2 * ND * 512, BF16)
        WUQ = AR.take(2 * ND * 512, BF16)
        WDQ = AR.take(2 * 1024, BF16)
        p.wait("sy", s_gd, s_gd.n)
        p.wait("sy", s_ve, s_ve.n)
        vx = p.dma("sy", XQF[:], sXQ, s_wx)
        p.wait("te", s_wx, vx)

        swg = STAT[:, 48:49]
        swu = STAT[:, 49:50]
        swd = STAT[:, 50:51]
        vwgq = [0, 0]
        vwuq = [0, 0]
        vte_oc = {}
        vve_ffn = 0
        for tt in range(NT):
            ab_col = STAT[:, 64 + tt:65 + tt]
            p.emit("ve", lambda e, ab_col=ab_col: e.tensor_tensor(
                STAT[:, 51:52], ab_col, STAT[:, 56:57], ALU.mult))
            p.emit("ve", lambda e, ab_col=ab_col: e.tensor_tensor(
                STAT[:, 52:53], ab_col, STAT[:, 57:58], ALU.mult))
            vve_ffn = p.tick("ve", s_ve)
            for oc in range(ND):
                b = oc % 2
                p.wait("sy", s_te, vte_oc.get((tt, oc, "free"), 0))
                src = sWGQ[:, :, 512 * oc:512 * (oc + 1)].transpose([1, 0, 2])
                vwgq[b] = p.dma("sy", WGQ[:, 4096 * b:4096 * (b + 1)].rearrange(
                    "p (d c) -> p d c", d=ND), src, s_wx)
                src = sWUQ[:, :, 512 * oc:512 * (oc + 1)].transpose([1, 0, 2])
                vwuq[b] = p.dma("sy", WUQ[:, 4096 * b:4096 * (b + 1)].rearrange(
                    "p (d c) -> p d c", d=ND), src, s_wx)
                p.wait("te", s_wx, vwuq[b])
                p.wait("te", s_ve, vve_ffn)
                for mi, (WQ_, bk) in enumerate(((WGQ, 0), (WUQ, 1))):
                    for dc in range(ND):
                        p.emit("te", lambda e, WQ_=WQ_, bk=bk, b=b, dc=dc,
                               tt=tt: e.matmul(
                            bank(bk),
                            XQF[:, T * dc + 128 * tt:T * dc + 128 * (tt + 1)],
                            WQ_[:, 4096 * b + 512 * dc:4096 * b + 512 * (dc + 1)],
                            start=(dc == 0), stop=(dc == ND - 1)))
                vte = p.tick("te", s_te)
                vte_oc[(tt, oc + 2, "free")] = vte
                p.wait("ve", s_te, vte)
                p.emit("ve", lambda e, oc=oc: e.scalar_tensor_tensor(
                    GT[:, 512 * oc:512 * (oc + 1)], bank(0), 1.0,
                    STAT[:, 51:52], ALU.bypass, ALU.mult)
                    if False else e.tensor_scalar(
                    GT[:, 512 * oc:512 * (oc + 1)], bank(0),
                    STAT[:, 51:52], None, ALU.mult))
                p.emit("ve", lambda e, oc=oc: e.tensor_scalar(
                    UT[:, 512 * oc:512 * (oc + 1)], bank(1),
                    STAT[:, 52:53], None, ALU.mult))
                vve_ffn = p.tick("ve", s_ve)
            if a0stop == "f1g" and tt == 0:
                p.wait("ve", s_ve, vve_ffn)
                for i4 in range(4):
                    p.emit("ve", lambda e, i4=i4: e.tensor_copy(
                        out=XT[i4][:], in_=GT[:, 1024 * i4:1024 * (i4 + 1)]))
                for i4 in range(4):
                    p.emit("ve", lambda e, i4=i4: e.tensor_copy(
                        out=XT[4 + i4][:],
                        in_=UT[:, 1024 * i4:1024 * (i4 + 1)]))
                p.tick("ve", s_ve)
                return
            p.wait("sc", s_ve, vve_ffn)
            p.emit("sc", lambda e: e.activation(out=GT[:], in_=GT[:],
                                                func=AF.Silu))
            vsc = p.tick("sc", s_sc)
            p.wait("ve", s_sc, vsc)
            p.emit("ve", lambda e: e.tensor_tensor(UT[:], GT[:], UT[:],
                                                   ALU.mult))
            p.drain("ve", s_ve)
            p.emit("ve", lambda e: e.reduce_max(
                STAT[:, 54:55], UT[:], AX.X, apply_absolute_value=True))
            p.drain("ve", s_ve)
            p.emit("ve", lambda e: e.tensor_scalar(
                STAT[:, 55:56], STAT[:, 54:55], 1e-5, None, ALU.max))
            p.drain("ve", s_ve)
            p.emit("ve", lambda e: e.reciprocal(STAT[:, 59:60], STAT[:, 55:56]))
            p.drain("ve", s_ve)
            p.emit("ve", lambda e: e.tensor_scalar(
                STAT[:, 60:61], STAT[:, 55:56], 1.0 / 127.0, None, ALU.mult))
            p.emit("ve", lambda e: e.tensor_scalar(
                STAT[:, 54:55], STAT[:, 59:60], 127.0, None, ALU.mult))
            p.emit("ve", lambda e: e.tensor_tensor(
                STAT[:, 53:54], STAT[:, 60:61], STAT[:, 58:59], ALU.mult))
            qs = [slice(1024 * q, 1024 * (q + 1)) for q in range(4)]
            for q in qs:
                p.emit("ve", lambda e, q=q: e.tensor_scalar(
                    UT[:, q], UT[:, q], STAT[:, 54:55], MAGIC,
                    ALU.mult, ALU.add))
            for q in qs:
                p.emit("ve", lambda e, q=q: e.tensor_scalar(
                    UT[:, q], UT[:, q], -MAGIC, -128.0, ALU.add, ALU.max))
            for q in qs:
                p.emit("ve", lambda e, q=q: e.tensor_scalar(
                    UT[:, q], UT[:, q], 127.0, None, ALU.min))
            vve_ffn = p.tick("ve", s_ve)
            if a0stop == "f1p" and tt == 0:
                p.wait("ve", s_ve, vve_ffn)
                for i4 in range(4):
                    p.emit("ve", lambda e, i4=i4: e.tensor_copy(
                        out=XT[i4][:], in_=UT[:, 1024 * i4:1024 * (i4 + 1)]))
                p.emit("ve", lambda e: e.tensor_copy(
                    out=XT[4][:, 0:80], in_=STAT[:]))
                p.tick("ve", s_ve)
                return
            for g0 in range(0, 32, 7):
                gb = min(7, 32 - g0)
                p.wait("te", s_ve, vve_ffn)
                for j in range(gb):
                    p.emit("te", lambda e, g0=g0, j=j: e.transpose(
                        bank(j, 0, 128),
                        UT[:, 128 * (g0 + j):128 * (g0 + j + 1)], CEYEf[:]))
                fence_mm()
                vte = p.tick("te", s_te)
                p.wait("ve", s_te, vte)
                for j in range(gb):
                    def evacx(e, g0=g0, j=j):
                        return e.tensor_copy(
                            out=XQ2F[:, 128 * (g0 + j):128 * (g0 + j + 1)],
                            in_=bank(j, 0, 128))
                    p.emit("ve", evacx)
                vve_ffn = p.tick("ve", s_ve)
            vwd = [0, 0]
            vdc = [0, 0]
            for ic in range(32):
                b = ic % 2
                p.wait("sy", s_te, vdc[b])
                vwd[b] = p.dma("sy", WDQ[:, 1024 * b:1024 * (b + 1)],
                               sWDQ[ic, :, :], s_wx)
                p.wait("te", s_wx, vwd[b])
                if ic == 0:
                    p.wait("te", s_ve, vve_ffn)
                for oc2 in range(2):
                    p.emit("te", lambda e, ic=ic, oc2=oc2, b=b: e.matmul(
                        bank(2 + oc2),
                        XQ2F[:, 128 * ic:128 * (ic + 1)],
                        WDQ[:, 1024 * b + 512 * oc2:1024 * b + 512 * (oc2 + 1)],
                        start=(ic == 0), stop=(ic == 31)))
                vdc[b] = p.tick("te", s_te)
            p.wait("ve", s_te, max(vdc))
            for oc2 in range(2):
                p.emit("ve", lambda e, tt=tt, oc2=oc2: e.scalar_tensor_tensor(
                    XT[tt][:, 512 * oc2:512 * (oc2 + 1)],
                    bank(2 + oc2), STAT[:, 53:54],
                    XT[tt][:, 512 * oc2:512 * (oc2 + 1)],
                    ALU.mult, ALU.add))
            vve_ffn = p.tick("ve", s_ve)
            p.wait("te", s_ve, vve_ffn)

    if nphase >= 2:
        emit_ffn0()

    def emit_ln1(cc_cores):
        AR.reset()
        HIFM1 = None
        STG4b = AR.take(4096, BF16)
        emit_ln_stats(0, 16, 32, absmax_only=True)
        emit_split_fm2(STG4b, 0, 16, sHI, sLO)
        p.emit("ve", lambda e: e.memset(SCR[:, 0:128], 0.0))
        p.drain("ve", s_ve)
        p.emit("ve", lambda e: e.reduce_max(
            SCR[:, 0:1], STAT[:, 32:48], AX.X))
        vve = p.tick("ve", s_ve)
        p.wait("te", s_ve, vve)
        p.emit("te", lambda e: e.transpose(
            bank(3, 0, 128), SCR[:, 0:128], CEYEf[:]))
        fence_mm()
        vte = p.tick("te", s_te)
        p.wait("ve", s_te, vte)
        p.emit("ve", lambda e: e.reduce_max(
            SCR2[:, 0:1], bank(3, 0, 128), AX.X))
        p.drain("ve", s_ve)
        p.emit("ve", lambda e: e.tensor_scalar(
            SCR2[:, 4:36], ONESF[:, 0:32], SCR2[:, 0:1], None,
            ALU.mult))
        vve = p.tick("ve", s_ve)
        vcc = p.dma("gp", sCCI, SCR2[0:1, 4:36], s_gd, wait=(s_ve, vve))
        p.wait("gp", s_gd, vcc)

        def do_cc(e):
            e.collective_compute(
                "AllReduce", ALU.max,
                replica_groups=[list(range(cc_cores))],
                ins=[sCCI.opt()], outs=[sCCO.opt()]).then_inc(s_cc.h, 1)
        p.emit("gp", do_cc)
        s_cc.inc(1)
        vccd = p.dma("gp", SCR2[:, 0:1],
                     sCCO[0:1, 0:1].partition_broadcast(128), s_gd,
                     wait=(s_cc, s_cc.n))
        p.wait("ve", s_gd, vccd)
        p.emit("ve", lambda e: e.tensor_scalar(
            STAT[:, 48:49], SCR2[:, 0:1], 1.0 / 127.0, 1e-5,
            ALU.mult, ALU.max))
        p.drain("ve", s_ve)
        p.emit("ve", lambda e: e.reciprocal(STAT[:, 49:50], STAT[:, 48:49]))
        p.drain("ve", s_ve)

    def emit_split_fm2(STG, m0, r0, hbm_hi, hbm_lo):
        state = {"free": 0, "lastq": 0}
        HIS_ = STG[:, 0:1024]
        LOS_ = STG[:, 1024:2048]
        for t in range(NT):
            mcol = STAT[:, m0 + t:m0 + t + 1]
            rcol = STAT[:, r0 + t:r0 + t + 1]
            p.wait("ve", s_te, s_te.n)
            p.emit("ve", lambda e, t=t, mcol=mcol, rcol=rcol: e.tensor_scalar(
                SCR[:], XT[t][:], mcol, rcol, ALU.subtract, ALU.mult))
            vprep = p.tick("ve", s_ve)
            free = max(state["free"], vprep)
            for d0 in range(0, ND, 7):
                ds = list(range(d0, min(d0 + 7, ND)))
                p.wait("te", s_ve, free)
                for j, d in enumerate(ds):
                    p.emit("te", lambda e, j=j, d=d: e.transpose(
                        bank(j, 0, 128), SCR[:, 128 * d:128 * (d + 1)],
                        CEYEf[:]))
                fence_mm()
                vte = p.tick("te", s_te)
                p.wait("ve", s_te, vte)
                if d0 == 0:
                    p.wait("ve", s_gd, state["lastq"])
                for j, d in enumerate(ds):
                    def evh(e, j=j, d=d):
                        return e.tensor_copy(
                            out=HIS_[:, 128 * d:128 * (d + 1)],
                            in_=bank(j, 0, 128))
                    p.emit("ve", evh)
                for j, d in enumerate(ds):
                    def evl(e, j=j, d=d):
                        return e.tensor_tensor(
                            LOS_[:, 128 * d:128 * (d + 1)],
                            bank(j, 0, 128),
                            HIS_[:, 128 * d:128 * (d + 1)], ALU.subtract)
                    p.emit("ve", evl)
                free = p.tick("ve", s_ve)
            state["free"] = free
            vv = free
            for src2, dst2 in ((HIS_, hbm_hi), (LOS_, hbm_lo)):
                dst_ap = dst2.rearrange("p (d t2) -> p d t2", d=ND)[
                    :, :, 128 * t:128 * (t + 1)]
                src_ap2 = src2.rearrange("p (d c) -> p d c", d=ND)
                state["lastq"] = p.dma("gp", dst_ap, src_ap2, s_gd,
                                       wait=(s_ve, vv))

    def emit_l1_wsplit(hw_src, n_rowtiles, width, dst_h, dst_l):
        p.wait("sy", s_ve, s_ve.n)
        p.wait("sy", s_gd, s_gd.n)
        ar0 = AR.off
        LD = AR.take(width, F32)
        HB = AR.take(width, BF16)
        LB = AR.take(width, BF16)
        vwb = 0
        for rt in range(n_rowtiles):
            vld = p.dma("sy", LD[:], hw_src[128 * rt:128 * (rt + 1), :], s_wx)
            p.wait("ve", s_wx, vld)
            p.wait("ve", s_gd, vwb)
            wh2 = width // 2
            for hs in (slice(0, wh2), slice(wh2, width)):
                p.emit("ve", lambda e, HB=HB, LD=LD, hs=hs: e.tensor_copy(
                    out=HB[:, hs], in_=LD[:, hs]))
            for hs in (slice(0, wh2), slice(wh2, width)):
                p.emit("ve", lambda e, HB=HB, LB=LB, LD=LD, hs=hs:
                       e.scalar_tensor_tensor(
                    LB[:, hs], HB[:, hs], -1.0, LD[:, hs],
                    ALU.mult, ALU.add))
            vve = p.tick("ve", s_ve)
            p.dma("gp", dst_h[rt, :, :], HB[:], s_gd, wait=(s_ve, vve))
            vwb = p.dma("gp", dst_l[rt, :, :], LB[:], s_gd)
            p.wait("sy", s_ve, vve)
        p.wait("ve", s_gd, vwb)
        p.tick("ve", s_ve)
        AR.off = ar0

    def emit_w8t_prep():
        p.wait("sy", s_ve, s_ve.n)
        p.wait("sy", s_gd, s_gd.n)
        ar0 = AR.off
        LD8 = AR.take(1024, I8)
        B8 = AR.take(1024, F32)
        QS8 = AR.take(1024, BF16)
        vwb = 0
        for nt2 in range(8):
            vld = p.dma("sy", LD8[:, :], hwi8[128 * nt2:128 * (nt2 + 1), :],
                        s_wx)
            p.wait("ve", s_wx, vld)
            p.emit("ve", lambda e, B8=B8, LD8=LD8: e.tensor_copy(
                out=B8[:], in_=LD8[:]))
            vve = p.tick("ve", s_ve)
            for g0 in range(0, 8, 7):
                gb = min(7, 8 - g0)
                p.wait("te", s_ve, vve)
                for j in range(gb):
                    p.emit("te", lambda e, g0=g0, j=j, B8=B8: e.transpose(
                        bank(j, 0, 128),
                        B8[:, 128 * (g0 + j):128 * (g0 + j + 1)], CEYEf[:]))
                fence_mm()
                vte = p.tick("te", s_te)
                p.wait("ve", s_te, vte)
                if g0 == 0:
                    p.wait("ve", s_gd, vwb)
                for j in range(gb):
                    def evac8(e, g0=g0, j=j, QS8=QS8):
                        return e.tensor_copy(
                            out=QS8[:, 128 * (g0 + j):128 * (g0 + j + 1)],
                            in_=bank(j, 0, 128))
                    p.emit("ve", evac8)
                vve = p.tick("ve", s_ve)
            dst_ap = sW8T[:, :, 128 * nt2:128 * (nt2 + 1)].transpose([1, 0, 2])
            src_ap = QS8[:].rearrange("p (g c) -> p g c", g=8)
            vwb = p.dma("gp", dst_ap, src_ap, s_gd, wait=(s_ve, vve))
            p.wait("sy", s_ve, vve)
        p.wait("ve", s_gd, vwb)
        p.tick("ve", s_ve)
        AR.off = ar0

    def emit_attn1():
        AR.reset()
        XCH = AR.take(16 * SEG, BF16)
        XCL = AR.take(16 * SEG, BF16)
        AT = AR.take(16 * SEG, F32)
        SRES = AR.take(16 * SEG, F32)
        BOUTB = AR.take(C, F32)
        LNGB = CPOOL[:, 0:1024]
        LNBB = AR.take(C, F32)
        COLS = AR.take(16 * 4 + 64, F32)
        BINC = AR.take(32, F32)
        BDTC = AR.take(16, F32)
        DSKC = AR.take(16, F32)
        CONVB = AR.take(16, F32)
        XSTAIL = AR.take(2 * 16 * 3, F32)
        CARRY = AR.take(16, F32)
        XSW = AR.take(515 + 1, F32)
        STRM = AR.take(10 * 1024, BF16)

        vv = p.tick("ve", s_ve)
        p.dma("gp", BOUTB[:], hbout.partition_broadcast(128), s_gd,
              wait=(s_ve, vv))
        p.dma("gp", LNGB[:], hlng.partition_broadcast(128), s_gd)
        p.dma("gp", LNBB[:], hlnb.partition_broadcast(128), s_gd)
        p.dma("gp", BINC[:].rearrange("p c -> p c"),
              hbin.rearrange("a (c p) -> (a p) c", p=128), s_gd, slow=True)
        p.dma("gp", BDTC[:], hbdt.rearrange("a (c p) -> (a p) c", p=128),
              s_gd, slow=True)
        p.dma("gp", DSKC[:], hdskip.rearrange("a (c p) -> (a p) c", p=128),
              s_gd, slow=True)
        p.dma("gp", CONVB[:], hconvb.rearrange("a (c p) -> (a p) c", p=128),
              s_gd, slow=True)
        vcols = p.dma("gp", COLS[:, 0:64].rearrange("p (c k) -> p c k", k=4),
                      hconvw.rearrange("(c p) k -> p c k", p=128), s_gd, slow=True)
        p.wait("ve", s_gd, vcols)
        p.emit("ve", lambda e: e.memset(XSTAIL[:], 0.0))
        p.emit("ve", lambda e: e.memset(CARRY[:], 0.0))
        vve_a = p.tick("ve", s_ve)
        p.wait("sc", s_ve, vve_a)
        p.wait("te", s_ve, vve_a)

        HIS = STRM[:, 0:4096]
        LOS = STRM[:, 4096:8192]
        WIH = STRM[:, 8192:9216]
        WIL = STRM[:, 9216:10240]
        WDH = STRM[:, 0:2048]
        WDL = STRM[:, 2048:4096]
        WOH = STRM[:, 0:8192]
        WOL = STRM[:, 8192:10240]

        for sg in range(NSEG):
            p.wait("sy", s_ve, s_ve.n)
            vhi = p.dma("sy", HIS.rearrange("p (d c) -> p d c", d=ND),
                        sHI.rearrange("p (d t2) -> p d t2", d=ND)[
                            :, :, SEG * sg:SEG * (sg + 1)], s_wx)
            vlo = p.dma("sy", LOS.rearrange("p (d c) -> p d c", d=ND),
                        sLO.rearrange("p (d t2) -> p d t2", d=ND)[
                            :, :, SEG * sg:SEG * (sg + 1)], s_wx)
            vwi_last = 0
            xsw_free = 0
            for chc in range(32):
                src_h = sWINH[:, :, 128 * chc:128 * (chc + 1)].transpose(
                    [1, 0, 2])
                src_l = sWINL[:, :, 128 * chc:128 * (chc + 1)].transpose(
                    [1, 0, 2])
                p.wait("sy", s_te, vwi_last)
                v1 = p.dma("sy", WIH.rearrange("p (d c) -> p d c", d=ND),
                           src_h, s_wx)
                v2 = p.dma("sy", WIL.rearrange("p (d c) -> p d c", d=ND),
                           src_l, s_wx)
                p.wait("te", s_wx, max(v2, vhi, vlo))
                for ip, (wop, xop) in enumerate(
                        ((WIH, HIS), (WIH, LOS), (WIL, HIS))):
                    for dc in range(ND):
                        p.emit("te", lambda e, wop=wop, xop=xop, dc=dc,
                               ip=ip: e.matmul(
                            bank(0),
                            wop[:, 128 * dc:128 * (dc + 1)],
                            xop[:, 512 * dc:512 * (dc + 1)],
                            start=(ip == 0 and dc == 0),
                            stop=(ip == 2 and dc == ND - 1)))
                vte = p.tick("te", s_te)
                vwi_last = vte
                p.wait("sc", s_te, vte)
                if chc < 16:
                    p.wait("sc", s_ve, xsw_free)
                    p.emit("sc", lambda e, chc=chc: e.activation(
                        out=XSW[:, 3:515], in_=bank(0), func=AF.Silu,
                        bias=BINC[:, chc:chc + 1]))
                    vsc = p.tick("sc", s_sc)
                    p.wait("ve", s_sc, vsc)
                    tin = 48 * (sg % 2)
                    tout = 48 * (1 - sg % 2)
                    p.emit("ve", lambda e, chc=chc, tin=tin: e.tensor_copy(
                        out=XSW[:, 0:3],
                        in_=XSTAIL[:, tin + 3 * chc:tin + 3 * chc + 3]))
                    p.emit("ve", lambda e, chc=chc, tout=tout: e.tensor_copy(
                        out=XSTAIL[:, tout + 3 * chc:tout + 3 * chc + 3],
                        in_=XSW[:, 512:515]))
                    vve = p.tick("ve", s_ve)
                    hc = (slice(0, 256), slice(256, 512))
                    for hs in hc:
                        p.emit("ve", lambda e, chc=chc, hs=hs: e.tensor_scalar(
                            SCR[:, hs], XSW[:, hs],
                            COLS[:, 4 * chc:4 * chc + 1], None, ALU.mult))
                    for k in range(1, 4):
                        for hs in hc:
                            p.emit("ve", lambda e, chc=chc, k=k, hs=hs:
                                   e.scalar_tensor_tensor(
                                SCR[:, hs], XSW[:, k + hs.start:k + hs.stop],
                                COLS[:, 4 * chc + k:4 * chc + k + 1],
                                SCR[:, hs], ALU.mult, ALU.add))
                    vve = p.tick("ve", s_ve)
                    xsw_free = vve
                    p.wait("sc", s_ve, vve)
                    p.emit("sc", lambda e, chc=chc: e.activation(
                        out=SCR2[:, 0:512], in_=SCR[:, 0:512], func=AF.Silu,
                        bias=CONVB[:, chc:chc + 1]))
                    vsc = p.tick("sc", s_sc)
                    p.wait("ve", s_sc, vsc)
                    for hs in hc:
                        p.emit("ve", lambda e, chc=chc, hs=hs: e.tensor_copy(
                            out=XCH[:, 512 * chc + hs.start:
                                     512 * chc + hs.stop],
                            in_=SCR2[:, hs]))
                    for hs in hc:
                        p.emit("ve", lambda e, chc=chc, hs=hs:
                               e.scalar_tensor_tensor(
                            XCL[:, 512 * chc + hs.start:512 * chc + hs.stop],
                            XCH[:, 512 * chc + hs.start:512 * chc + hs.stop],
                            -1.0, SCR2[:, hs], ALU.mult, ALU.add))
                    vve = p.tick("ve", s_ve)
                else:
                    rc = chc - 16
                    p.emit("sc", lambda e, chc=chc, rc=rc: e.activation(
                        out=SRES[:, 512 * rc:512 * (rc + 1)], in_=bank(0),
                        func=AF.Silu, bias=BINC[:, chc:chc + 1]))
                    vsc = p.tick("sc", s_sc)
                p.wait("te", s_sc, vsc)

            vdt_last = 0
            for co in range(16):
                p.wait("sy", s_te, vdt_last)
                v1 = p.dma("sy", WDH.rearrange("p (d c) -> p d c", d=16),
                           sWDTH[:, :, 128 * co:128 * (co + 1)].transpose(
                               [1, 0, 2]), s_wx)
                v2 = p.dma("sy", WDL.rearrange("p (d c) -> p d c", d=16),
                           sWDTL[:, :, 128 * co:128 * (co + 1)].transpose(
                               [1, 0, 2]), s_wx)
                p.wait("te", s_wx, v2)
                for ip, (wop, xop) in enumerate(
                        ((WDH, XCH), (WDH, XCL), (WDL, XCH))):
                    for ci in range(16):
                        p.emit("te", lambda e, wop=wop, xop=xop, ci=ci,
                               ip=ip: e.matmul(
                            bank(1),
                            wop[:, 128 * ci:128 * (ci + 1)],
                            xop[:, 512 * ci:512 * (ci + 1)],
                            start=(ip == 0 and ci == 0),
                            stop=(ip == 2 and ci == 15)))
                vte = p.tick("te", s_te)
                vdt_last = vte
                p.wait("ve", s_te, vte)
                p.emit("ve", lambda e, co=co: e.tensor_scalar(
                    SCR[:, 0:512], bank(1), BDTC[:, co:co + 1], 80.0,
                    ALU.add, ALU.min))
                vve = p.tick("ve", s_ve)
                p.wait("sc", s_ve, vve)
                p.emit("sc", lambda e: e.activation(
                    out=SCR2[:, 0:512], in_=SCR[:, 0:512], func=AF.Exp))
                vsc = p.tick("sc", s_sc)
                p.wait("ve", s_sc, vsc)
                p.emit("ve", lambda e: e.tensor_scalar(
                    SCR2[:, 0:512], SCR2[:, 0:512], 2.0, None, ALU.add))
                p.drain("ve", s_ve)
                p.emit("ve", lambda e, co=co: e.reciprocal(
                    AT[:, 512 * co:512 * (co + 1)], SCR2[:, 0:512]))
                p.drain("ve", s_ve)
                vve = p.tick("ve", s_ve)
                p.wait("te", s_ve, vve)

            for ci in range(16):
                xch = XCH[:, 512 * ci:512 * (ci + 1)]
                xcl = XCL[:, 512 * ci:512 * (ci + 1)]
                at = AT[:, 512 * ci:512 * (ci + 1)]
                hc = (slice(0, 256), slice(256, 512))
                for hs in hc:
                    p.emit("ve", lambda e, xch=xch, xcl=xcl, hs=hs:
                           e.tensor_tensor(SCR[:, hs], xch[:, hs],
                                           xcl[:, hs], ALU.add))
                for hs in hc:
                    p.emit("ve", lambda e, at=at, hs=hs: e.tensor_tensor(
                        SCR2[:, hs], at[:, hs], SCR[:, hs], ALU.mult))
                for hs in hc:
                    p.emit("ve", lambda e, hs=hs: e.tensor_tensor(
                        SCR2[:, 512 + hs.start:512 + hs.stop],
                        SCR[:, hs], SCR2[:, hs], ALU.subtract))
                if sg == 0:
                    p.emit("ve", lambda e: e.tensor_copy(
                        out=SCR2[:, 512:513], in_=SCR[:, 0:1]))
                p.drain("ve", s_ve)
                p.emit("ve", lambda e, at=at, ci=ci: e.tensor_tensor_scan(
                    SCR2[:, 0:512], at, SCR2[:, 512:1024],
                    CARRY[:, ci:ci + 1], ALU.mult, ALU.add))
                p.drain("ve", s_ve)
                p.emit("ve", lambda e, ci=ci: e.tensor_copy(
                    out=CARRY[:, ci:ci + 1], in_=SCR2[:, 511:512]))
                for hs in hc:
                    p.emit("ve", lambda e, ci=ci, hs=hs:
                           e.scalar_tensor_tensor(
                        SCR[:, 512 + hs.start:512 + hs.stop], SCR[:, hs],
                        DSKC[:, ci:ci + 1], SCR2[:, hs],
                        ALU.mult, ALU.add))
                for hs in hc:
                    p.emit("ve", lambda e, ci=ci, hs=hs: e.tensor_tensor(
                        SCR[:, 512 + hs.start:512 + hs.stop],
                        SCR[:, 512 + hs.start:512 + hs.stop],
                        SRES[:, 512 * ci + hs.start:512 * ci + hs.stop],
                        ALU.mult))
                for hs in hc:
                    p.emit("ve", lambda e, xch=xch, hs=hs: e.tensor_copy(
                        out=xch[:, hs],
                        in_=SCR[:, 512 + hs.start:512 + hs.stop]))
                for hs in hc:
                    p.emit("ve", lambda e, xch=xch, xcl=xcl, hs=hs:
                           e.scalar_tensor_tensor(
                        xcl[:, hs], xch[:, hs], -1.0,
                        SCR[:, 512 + hs.start:512 + hs.stop],
                        ALU.mult, ALU.add))
            vscan = p.tick("ve", s_ve)

            p.wait("te", s_ve, vscan)
            for oc in range(2):
                p.wait("sy", s_ve, vscan)
                p.wait("sy", s_te, s_te.n)
                v1 = p.dma("sy", WOH.rearrange("p (d c) -> p d c", d=16),
                           sWOH[:, :, 512 * oc:512 * (oc + 1)].transpose(
                               [1, 0, 2]), s_wx)
                p.wait("te", s_wx, v1)
                for ts_ in range(4):
                    bk = (4 + ts_) if oc == 0 else ts_
                    for ip, xop in enumerate((XCH, XCL)):
                        for ci in range(16):
                            p.emit("te", lambda e, xop=xop, ci=ci,
                                   bk=bk, ts_=ts_, ip=ip: e.matmul(
                                bank(bk),
                                xop[:, 512 * ci + 128 * ts_:
                                    512 * ci + 128 * (ts_ + 1)],
                                WOH[:, 512 * ci:512 * (ci + 1)],
                                start=(ip == 0 and ci == 0), stop=False))
                vteh = p.tick("te", s_te)
                p.wait("sy", s_te, vteh)
                v2 = p.dma("sy", WOH.rearrange("p (d c) -> p d c", d=16),
                           sWOL[:, :, 512 * oc:512 * (oc + 1)].transpose(
                               [1, 0, 2]), s_wx)
                p.wait("te", s_wx, v2)
                for ts_ in range(4):
                    bk = (4 + ts_) if oc == 0 else ts_
                    for ci in range(16):
                        p.emit("te", lambda e, ci=ci, bk=bk, ts_=ts_: e.matmul(
                            bank(bk),
                            XCH[:, 512 * ci + 128 * ts_:
                                512 * ci + 128 * (ts_ + 1)],
                            WOH[:, 512 * ci:512 * (ci + 1)],
                            start=False, stop=(ci == 15)))
                vte = p.tick("te", s_te)
            for ts_ in range(4):
                t = 4 * sg + ts_
                p.wait("ve", s_te, vte)
                mcol = STAT[:, 0 + t:1 + t]
                rcol = STAT[:, 16 + t:17 + t]
                hc = (slice(0, 512), slice(512, 1024))
                for hs in hc:
                    p.emit("ve", lambda e, t=t, mcol=mcol, rcol=rcol, hs=hs:
                           e.tensor_scalar(
                        SCR[:, hs], XT[t][:, hs], mcol, rcol,
                        ALU.subtract, ALU.mult))
                for oc in range(2):
                    bk = (4 + ts_) if oc == 0 else ts_
                    p.emit("ve", lambda e, oc=oc, bk=bk: e.tensor_tensor(
                        SCR2[:, 512 * oc:512 * (oc + 1)], bank(bk),
                        SCR[:, 512 * oc:512 * (oc + 1)], ALU.add))
                for hs in hc:
                    p.emit("ve", lambda e, hs=hs: e.tensor_tensor(
                        SCR2[:, hs], SCR2[:, hs], BOUTB[:, hs], ALU.add))
                p.drain("ve", s_ve)
                p.emit("ve", lambda e: e.reduce_sum(
                    STAT[:, 50:51], SCR2[:], AX.X))
                vve = p.tick("ve", s_ve)
                p.wait("sc", s_ve, vve)
                p.emit("sc", lambda e: e.activation(
                    out=SCR[:], in_=SCR2[:], func=AF.Square,
                    accum_out=STAT[:, 51:52]))
                vsc = p.tick("sc", s_sc)
                p.drain("ve", s_ve)
                p.wait("ve", s_sc, vsc)
                p.emit("ve", lambda e: e.tensor_scalar(
                    STAT[:, 50:51], STAT[:, 50:51], 1.0 / C, None, ALU.mult))
                p.emit("ve", lambda e: e.tensor_scalar(
                    STAT[:, 51:52], STAT[:, 51:52], 1.0 / C, None, ALU.mult))
                p.emit("ve", lambda e: e.tensor_tensor(
                    STAT[:, 52:53], STAT[:, 50:51], STAT[:, 50:51], ALU.mult))
                p.drain("ve", s_ve)
                p.emit("ve", lambda e: e.tensor_tensor(
                    STAT[:, 51:52], STAT[:, 51:52], STAT[:, 52:53],
                    ALU.subtract))
                vve = p.tick("ve", s_ve)
                p.wait("sc", s_ve, vve)
                p.emit("sc", lambda e: e.activation(
                    out=STAT[:, 51:52], in_=STAT[:, 51:52], func=AF.Sqrt,
                    bias=EPS[:]))
                vsc = p.tick("sc", s_sc)
                p.wait("ve", s_sc, vsc)
                p.emit("ve", lambda e: e.reciprocal(
                    STAT[:, 52:53], STAT[:, 51:52]))
                p.drain("ve", s_ve)
                for hs in hc:
                    p.emit("ve", lambda e, hs=hs: e.tensor_scalar(
                        SCR2[:, hs], SCR2[:, hs], STAT[:, 50:51],
                        STAT[:, 52:53], ALU.subtract, ALU.mult))
                for hs in hc:
                    p.emit("ve", lambda e, hs=hs: e.tensor_tensor(
                        SCR2[:, hs], SCR2[:, hs], LNGB[:, hs], ALU.mult))
                for hs in hc:
                    p.emit("ve", lambda e, hs=hs: e.tensor_tensor(
                        SCR2[:, hs], SCR2[:, hs], LNBB[:, hs], ALU.add))
                for hs in hc:
                    p.emit("ve", lambda e, t=t, hs=hs: e.tensor_tensor(
                        XT[t][:, hs], XT[t][:, hs], SCR2[:, hs], ALU.add))
                vve = p.tick("ve", s_ve)
            p.wait("te", s_ve, vve)

    def emit_ffn1():
        AR.reset()
        XQ1F = AR.take(ND * T, BF16)
        W8S = AR.take(ND * 1024, BF16)
        WSXS = AR.take(C, F32)
        BIAS1B = AR.take(C, F32)
        HSB = AR.take(2048, BF16)
        LSB = AR.take(2048, BF16)
        TMPQ = AR.take(2048, F32)
        p.wait("sy", s_ve, s_ve.n)
        p.wait("sy", s_gd, s_gd.n)
        vw8 = p.dma("sy", W8S.rearrange("p (d c) -> p d c", d=ND),
                    sW8T.transpose([1, 0, 2]), s_wx)
        vv = p.tick("ve", s_ve)
        p.dma("gp", WSXS[:], hws.rearrange("n a -> a n").partition_broadcast(
            128), s_gd, wait=(s_ve, vv))
        vb1 = p.dma("gp", BIAS1B[:], hbias1.partition_broadcast(128), s_gd)
        p.wait("ve", s_gd, vb1)
        p.emit("ve", lambda e: e.tensor_scalar(
            WSXS[:], WSXS[:], STAT[:, 48:49], None, ALU.mult))
        for i in range(NT):
            p.emit("ve", lambda e, i=i: e.tensor_tensor(
                XT[i][:], XT[i][:], BIAS1B[:], ALU.add))
        vve = p.tick("ve", s_ve)
        vq = 0
        for dc in range(ND):
            vh = p.dma("sy", HSB[:],
                       sHI[:, T * dc:T * (dc + 1)], s_wx)
            vl = p.dma("sy", LSB[:],
                       sLO[:, T * dc:T * (dc + 1)], s_wx)
            p.wait("ve", s_wx, vl)
            hq = (slice(0, 1024), slice(1024, 2048))
            for hs in hq:
                p.emit("ve", lambda e, hs=hs: e.tensor_tensor(
                    TMPQ[:, hs], HSB[:, hs], LSB[:, hs], ALU.add))
            for hs in hq:
                p.emit("ve", lambda e, hs=hs: e.tensor_scalar(
                    TMPQ[:, hs], TMPQ[:, hs], STAT[:, 49:50], MAGIC,
                    ALU.mult, ALU.add))
            for hs in hq:
                p.emit("ve", lambda e, hs=hs: e.tensor_scalar(
                    TMPQ[:, hs], TMPQ[:, hs], -MAGIC, -128.0,
                    ALU.add, ALU.max))
            for hs in hq:
                p.emit("ve", lambda e, dc=dc, hs=hs: e.tensor_scalar(
                    XQ1F[:, T * dc + hs.start:T * dc + hs.stop],
                    TMPQ[:, hs], 127.0, None, ALU.min))
            vq = p.tick("ve", s_ve)
            p.wait("sy", s_ve, vq)
        p.wait("te", s_wx, vw8)
        p.wait("te", s_ve, vq)
        for tt in range(NT):
            for n2 in range(2):
                for dc in range(ND):
                    p.emit("te", lambda e, tt=tt, n2=n2, dc=dc: e.matmul(
                        bank(n2),
                        XQ1F[:, T * dc + 128 * tt:T * dc + 128 * (tt + 1)],
                        W8S[:, 1024 * dc + 512 * n2:1024 * dc + 512 * (n2 + 1)],
                        start=(dc == 0), stop=(dc == ND - 1)))
            vte = p.tick("te", s_te)
            p.wait("ve", s_te, vte)
            for n2 in range(2):
                p.emit("ve", lambda e, tt=tt, n2=n2: e.tensor_tensor(
                    SCR[:, 512 * n2:512 * (n2 + 1)], bank(n2),
                    WSXS[:, 512 * n2:512 * (n2 + 1)], ALU.mult))
            for n2 in range(2):
                p.emit("ve", lambda e, tt=tt, n2=n2: e.tensor_tensor(
                    XT[tt][:, 512 * n2:512 * (n2 + 1)],
                    XT[tt][:, 512 * n2:512 * (n2 + 1)],
                    SCR[:, 512 * n2:512 * (n2 + 1)], ALU.add))
            vve = p.tick("ve", s_ve)
            p.wait("te", s_ve, vve)

    if nphase >= 3:
        emit_ln1(cc_cores)
        emit_l1_wsplit(hwin, ND, 2 * DI, sWINH, sWINL)
        emit_l1_wsplit(hwdt, 16, DI, sWDTH, sWDTL)
        emit_l1_wsplit(hwout, 16, C, sWOH, sWOL)
        emit_w8t_prep()
    if nphase >= 4:
        emit_attn1()
    if nphase >= 5:
        emit_ffn1()

    ve_done = p.tick("ve", s_ve)
    for i in range(NT):
        p.dma("gp", hout[128 * i:128 * (i + 1), :], XT[i][:], s_st,
              wait=(s_ve, ve_done) if i == 0 else None)
    p.wait("gp", s_st, s_st.n)
    p.finalize()
    return nc


def kernel(L0_attn_x, L0_attn_Wq, L0_attn_Wk, L0_attn_Wv, L0_attn_Wp,
           L0_attn_bp, L0_ffn_w_gate, L0_ffn_w_up, L0_ffn_w_down,
           L1_attn_w_in, L1_attn_b_in, L1_attn_conv_w, L1_attn_conv_b,
           L1_attn_w_xp, L1_attn_b_xp, L1_attn_w_dt, L1_attn_b_dt,
           L1_attn_A_log, L1_attn_D_skip, L1_attn_w_out, L1_attn_b_out,
           L1_attn_ln_g, L1_attn_ln_b, L1_ffn_weight_int8,
           L1_ffn_weight_scale, L1_ffn_bias, _nc=None, _run=None):
    consts = build_constants()
    nc = _nc if _nc is not None else build_program()
    base = {
        "Wq": np.ascontiguousarray(L0_attn_Wq),
        "Wk": np.ascontiguousarray(L0_attn_Wk),
        "Wv": np.ascontiguousarray(L0_attn_Wv),
        "Wp": np.ascontiguousarray(L0_attn_Wp),
        "bp": np.ascontiguousarray(np.asarray(L0_attn_bp).reshape(1, C)),
        "wg": np.ascontiguousarray(L0_ffn_w_gate),
        "wu": np.ascontiguousarray(L0_ffn_w_up),
        "wd": np.ascontiguousarray(L0_ffn_w_down),
        "win": np.ascontiguousarray(L1_attn_w_in),
        "bin": np.ascontiguousarray(np.asarray(L1_attn_b_in).reshape(1, 2 * DI)),
        "convw": np.ascontiguousarray(np.asarray(L1_attn_conv_w).reshape(DI, 4)),
        "convb": np.ascontiguousarray(np.asarray(L1_attn_conv_b).reshape(1, DI)),
        "wdt": np.ascontiguousarray(L1_attn_w_dt),
        "bdt": np.ascontiguousarray(np.asarray(L1_attn_b_dt).reshape(1, DI)),
        "dskip": np.ascontiguousarray(np.asarray(L1_attn_D_skip).reshape(1, DI)),
        "wout": np.ascontiguousarray(L1_attn_w_out),
        "bout": np.ascontiguousarray(np.asarray(L1_attn_b_out).reshape(1, C)),
        "lng": np.ascontiguousarray(np.asarray(L1_attn_ln_g).reshape(1, C)),
        "lnb": np.ascontiguousarray(np.asarray(L1_attn_ln_b).reshape(1, C)),
        "wi8": np.ascontiguousarray(L1_ffn_weight_int8),
        "ws": np.ascontiguousarray(L1_ffn_weight_scale),
        "bias1": np.ascontiguousarray(np.asarray(L1_ffn_bias).reshape(1, C)),
    }
    base.update(consts)
    x = np.ascontiguousarray(np.asarray(L0_attn_x, dtype=np.float32))
    in_maps = []
    for b in range(B):
        m = dict(base)
        m["x"] = np.ascontiguousarray(x[b])
        in_maps.append(m)
    if _run is not None:
        return _run(nc, in_maps)
    res = run_bass_kernel_spmd(nc, in_maps, core_ids=list(range(B)))
    out = np.stack([np.asarray(r["out"]) for r in res.results], axis=0)
    return out.astype(np.float32)
